# version 02
# baseline (speedup 1.0000x reference)
import math
import numpy as np
import ml_dtypes
import concourse.bass as bass
import concourse.mybir as mybir
from concourse.bass_utils import run_bass_kernel_spmd

F32 = mybir.dt.float32
BF16 = mybir.dt.bfloat16
AF = mybir.ActivationFunctionType
ALU = mybir.AluOpType
AX = mybir.AxisListType.X
BFNP = ml_dtypes.bfloat16

B_, T, C, H, D = 2, 8192, 2048, 16, 128
NQ = 2048
NJ = 16
EPS = 1e-5
SLOPES = [2.0 ** (-0.5 * (i + 1)) for i in range(H)]


class Buf:
    __slots__ = ("w", "rs")
    def __init__(self):
        self.w = None
        self.rs = []


class Sched:
    ENG = ("sync", "gpsimd", "tensor", "vector", "scalar")
    def __init__(self):
        self.q = {e: [] for e in self.ENG}
        self.shadow = {"ld": 0, "st": 0, "gp": 0, "t": 0, "v": 0, "s": 0, "ls": 0}
        self.last_wait = {e: {} for e in self.ENG}
        self.bar = []

    def emit(self, eng, sem, inc, build, reads=(), writes=()):
        deps = list(self.bar)
        for b in reads:
            if b.w is not None:
                deps.append(b.w)
        for b in writes:
            if b.w is not None:
                deps.append(b.w)
            deps.extend(b.rs)
        lw = self.last_wait[eng]
        for (ds, dv) in deps:
            if eng == "tensor" and ds == "t":
                continue
            if lw.get(ds, 0) >= dv:
                continue
            lw[ds] = dv
            self.q[eng].append(("w", ds, dv))
        self.shadow[sem] += inc
        self.q[eng].append(("i", build, sem, inc))
        tok = (sem, self.shadow[sem])
        for b in reads:
            b.rs.append(tok)
        for b in writes:
            b.w = tok
            b.rs = []
        return tok

    def barrier(self):
        self.bar = [(s, v) for s, v in self.shadow.items() if v > 0]


def build_nc():
    nc = bass.Bass("TRN2", target_bir_lowering=False)
    S = Sched()

    d_xkv = nc.dram_tensor("xkv", [T, C], F32, kind="ExternalInput").ap()
    d_xq = nc.dram_tensor("xq", [NQ, C], F32, kind="ExternalInput").ap()
    d_wq = nc.dram_tensor("wq", [C, C], F32, kind="ExternalInput").ap()
    d_wk = nc.dram_tensor("wk", [C, C], F32, kind="ExternalInput").ap()
    d_wv = nc.dram_tensor("wv", [C, C], F32, kind="ExternalInput").ap()
    d_wp = nc.dram_tensor("wp", [C, C], F32, kind="ExternalInput").ap()
    d_qw = nc.dram_tensor("qw", [D], F32, kind="ExternalInput").ap()
    d_kw = nc.dram_tensor("kw", [D], F32, kind="ExternalInput").ap()
    d_sw = nc.dram_tensor("sw", [D], F32, kind="ExternalInput").ap()
    d_w1 = nc.dram_tensor("w1", [4096, 2048], F32, kind="ExternalInput").ap()
    d_b1 = nc.dram_tensor("b1", [4096], F32, kind="ExternalInput").ap()
    d_w2 = nc.dram_tensor("w2", [4096, 4096], F32, kind="ExternalInput").ap()
    d_b2 = nc.dram_tensor("b2", [4096], F32, kind="ExternalInput").ap()
    d_w3 = nc.dram_tensor("w3", [2048, 4096], F32, kind="ExternalInput").ap()
    d_b3 = nc.dram_tensor("b3bc", [128, 2048], F32, kind="ExternalInput").ap()
    d_mst = nc.dram_tensor("master", [128, 8192], BF16, kind="ExternalInput").ap()
    d_idb = nc.dram_tensor("idbf", [128, 128], BF16, kind="ExternalInput").ap()
    d_idf = nc.dram_tensor("idf32", [128, 128], F32, kind="ExternalInput").ap()
    d_onc = nc.dram_tensor("onec", [128, 1], BF16, kind="ExternalInput").ap()
    d_onr = nc.dram_tensor("oner", [1, 128], BF16, kind="ExternalInput").ap()
    d_out = nc.dram_tensor("out", [NQ, C], F32, kind="ExternalOutput").ap()
    d_kts = nc.dram_tensor("kts", [H, 128, T], BF16, kind="Internal").ap()
    d_vs = nc.dram_tensor("vs", [H, T, D], BF16, kind="Internal").ap()
    d_yts = nc.dram_tensor("yts", [H, 128, NQ], BF16, kind="Internal").ap()
    d_xrs = nc.dram_tensor("xrs", [NQ, C], F32, kind="Internal").ap()

    ctx_list = []
    def alloc(cm):
        h = cm.__enter__()
        ctx_list.append(cm)
        return h

    A = alloc(nc.sbuf_tensor("Ab", [128, 16384], F32))
    Bt = alloc(nc.sbuf_tensor("Bb", [128, 16384], F32))
    Ct = alloc(nc.sbuf_tensor("Cb", [128, 4096], F32))
    Dt = alloc(nc.sbuf_tensor("Db", [128, 8192], F32))
    Et = alloc(nc.sbuf_tensor("Eb", [128, 2048], F32))
    Ft = alloc(nc.sbuf_tensor("Fb", [128, 1792], F32))
    Af = A[:]; Bf = Bt[:]; Cf = Ct[:]; Df = Dt[:]; Ef = Et[:]; Ff = Ft[:]
    Abf = Af.bitcast(BF16); Bbf = Bf.bitcast(BF16); Cbf = Cf.bitcast(BF16)
    Dbf = Df.bitcast(BF16); Ebf = Ef.bitcast(BF16); Fbf = Ff.bitcast(BF16)

    ps = [alloc(nc.psum_tensor(f"ps{i}", [128, 512], F32)) for i in range(6)]
    pbm = [alloc(nc.psum_tensor(f"pb{i}", [128, 1024], BF16)) for i in range(2)]
    psf = [p[:] for p in ps]
    pbf = [p[:] for p in pbm]

    sems = {s: alloc(nc.semaphore(f"sem_{s}"))
            for s in ("ld", "st", "gp", "t", "v", "s", "ls")}

    bufs = {}
    def bf(name):
        if name not in bufs:
            bufs[name] = Buf()
        return bufs[name]

    _ldrr = [0]
    def LD(dst, src, rd=(), wr=()):
        _ldrr[0] ^= 1
        eng, sem = ("sync", "ld") if _ldrr[0] else ("scalar", "ls")
        return S.emit(eng, sem, 16,
                      lambda e, dst=dst, src=src: e.dma_start(out=dst, in_=src),
                      reads=rd, writes=wr)
    def GLD(dst, src, rd=(), wr=()):
        return S.emit("gpsimd", "st", 16,
                      lambda e, dst=dst, src=src: e.dma_start(out=dst, in_=src),
                      reads=rd, writes=wr)
    ST = GLD
    def MM(out, l, rh, st_, sp, rd=(), wr=()):
        return S.emit("tensor", "t", 1,
                      lambda e, out=out, l=l, rh=rh, st_=st_, sp=sp:
                      e.matmul(out, l, rh, start=st_, stop=sp,
                               skip_group_check=True),
                      reads=rd, writes=wr)
    def TR(out, in_, ident, rd=(), wr=()):
        return S.emit("tensor", "t", 1,
                      lambda e, out=out, in_=in_, ident=ident:
                      e.transpose(out, in_, ident),
                      reads=rd, writes=wr)
    def V(buildf, rd=(), wr=()):
        return S.emit("vector", "v", 1, buildf, reads=rd, writes=wr)
    def SC(buildf, rd=(), wr=()):
        return S.emit("scalar", "s", 1, buildf, reads=rd, writes=wr)
    def GP(buildf, rd=(), wr=()):
        return S.emit("gpsimd", "gp", 1, buildf, reads=rd, writes=wr)

    qw_c = Ff[:, 0:1]; kw_c = Ff[:, 1:2]; sw_c = Ff[:, 2:3]
    wqs = Ff[:, 3:19]
    maxc = [Ff[:, 19:20], Ff[:, 28:29]]
    negm = [Ff[:, 20:21], Ff[:, 29:30]]
    denc = [Ff[:, 21:22], Ff[:, 30:31]]
    denp = [[Ff[:, 22:23], Ff[:, 23:24]], [Ff[:, 31:32], Ff[:, 32:33]]]
    b1c_ = [Ff[:, 24:25], Ff[:, 25:26]]
    b2c_ = [Ff[:, 26:27], Ff[:, 27:28]]
    LNC = [(33, 34, 35, 36, 37, 38, 39), (40, 41, 42, 43, 44, 45, 46)]
    idf_sb = Ff[:, 48:176]
    rstd1_b = Fbf[0:1, 352:864]
    rstd2_b = Fbf[0:1, 352:480]
    drow = Ff[0:1, 560:688]
    t1r = Ff[0:1, 688:816]
    msr = Ff[0:1, 816:944]
    rowS = Ff[0:1, 960:1472]
    idb_sb = Fbf[:, 3168:3296]
    onc_sb = Fbf[:, 3296:3297]
    onr_sb = Fbf[0:1, 3298:3426]

    eps_c = Ff[:, 1472:1473]
    sqeps_c = Ff[:, 1473:1474]
    i128_c = Ff[:, 1474:1475]
    slopes_c = Ff[:, 1475:1491]

    bID = bf("id"); bON = bf("ones"); bCOL = bf("cols")
    V(lambda e: e.memset(eps_c, EPS), wr=[bCOL])
    V(lambda e: e.memset(sqeps_c, float(math.sqrt(EPS))), wr=[bCOL])
    V(lambda e: e.memset(i128_c, 1.0 / 128.0), wr=[bCOL])
    for h in range(H):
        V(lambda e, h=h: e.memset(slopes_c[:, h:h + 1], float(SLOPES[h])),
          wr=[bCOL])
    LD(idb_sb, d_idb, wr=[bID])
    LD(idf_sb, d_idf, wr=[bID])
    LD(onc_sb, d_onc, wr=[bON])
    LD(onr_sb, d_onr, wr=[bON])
    LD(qw_c, d_qw[:, None], wr=[bCOL])
    LD(kw_c, d_kw[:, None], wr=[bCOL])
    LD(sw_c, d_sw[:, None], wr=[bCOL])
    for h in range(H):
        sc_h = 1.0 / (math.sqrt(D) * SLOPES[h])
        V(lambda e, h=h, sc_h=sc_h: e.tensor_scalar_mul(
            wqs[:, h:h + 1], qw_c, sc_h), rd=[bCOL], wr=[bf("wqs")])

    def layer_norm(xap, x2ap, outap, bx, bx2, bout, par):
        (c0, c1, c2, c3, c4, c5, c6) = LNC[par]
        cS = Ff[:, c0:c0 + 1]; cQ = Ff[:, c1:c1 + 1]; cM = Ff[:, c2:c2 + 1]
        cM2 = Ff[:, c3:c3 + 1]; cVar = Ff[:, c4:c4 + 1]
        cStd = Ff[:, c5:c5 + 1]; cNmr = Ff[:, c6:c6 + 1]
        bc = bf(f"lncol{par}")
        V(lambda e, a=cS, b=xap: e.reduce_sum(a, b, axis=AX), rd=[bx], wr=[bc])
        SC(lambda e, a=x2ap, b=xap: e.activation(a, b, AF.Square),
           rd=[bx], wr=[bx2])
        V(lambda e, a=cQ, b=x2ap: e.reduce_sum(a, b, axis=AX), rd=[bx2], wr=[bc])
        V(lambda e, a=cM, b=cS: e.tensor_scalar_mul(a, b, 1.0 / 2048.0),
          rd=[bc], wr=[bc])
        V(lambda e, a=cM2, b=cM: e.tensor_mul(a, b, b), rd=[bc], wr=[bc])
        V(lambda e, a=cVar, b=cQ, c=cM2: e.scalar_tensor_tensor(
            a, b, 1.0 / 2048.0, c, ALU.mult, ALU.subtract), rd=[bc], wr=[bc])
        SC(lambda e, a=cStd, b=cVar: e.activation(a, b, AF.Sqrt, bias=eps_c),
           rd=[bc, bCOL], wr=[bc])
        V(lambda e, a=cStd: e.reciprocal(a, a), rd=[bc], wr=[bc])
        V(lambda e, a=cNmr, b=cM, c=cStd: e.scalar_tensor_tensor(
            a, b, -1.0, c, ALU.mult, ALU.mult), rd=[bc], wr=[bc])
        SC(lambda e, a=outap, b=xap, bi=cNmr, sc=cStd: e.activation(
            a, b, AF.Identity, bias=bi, scale=sc), rd=[bx, bc], wr=[bout])

    def rms_apply(pp, bpp, outap, bout, sq_ap, bsq, w_col, bw, n):
        ssq = psf[3][0:1, 0:n]
        bss = bf("ps3")
        bro = bf("rmsrow")
        rS = rowS[:, 0:n]
        rB = rstd1_b[:, 0:n]
        bcsb = Ef[:, 512:1024][:, 0:n]
        bbcs = bf("bcsb1")
        SC(lambda e, a=sq_ap, b=pp: e.activation(a, b, AF.Square),
           rd=[bpp], wr=[bsq])
        MM(ssq, onc_sb, sq_ap, True, True, rd=[bON, bsq], wr=[bss])
        SC(lambda e, a=rS, b=ssq: e.activation(
            a, b, AF.Sqrt, scale=i128_c[0:1, :], bias=eps_c[0:1, :]), rd=[bss, bCOL], wr=[bro])
        V(lambda e, a=rB, b=rS: e.reciprocal(a, b), rd=[bro], wr=[bro])
        bcp = psf[4][:, 0:n]
        bbc = bf("ps4")
        MM(bcp, onr_sb, rB, True, True, rd=[bON, bro], wr=[bbc])
        SC(lambda e, a=bcsb, b=bcp: e.activation(a, b, AF.Copy),
           rd=[bbc], wr=[bbcs])
        V(lambda e, a=outap, b=pp, w=w_col, c=bcsb: e.scalar_tensor_tensor(
            a, b, w, c, ALU.mult, ALU.mult), rd=[bpp, bbcs, bw], wr=[bout])

    def load_w(dsrc, dest_bf, tag):
        for g in range(16):
            wb = bf(f"wload{g % 2}")
            cw = Cf[:, (g % 2) * 2048:(g % 2) * 2048 + 2048]
            LD(cw, dsrc[g * 128:(g + 1) * 128, :], wr=[wb])
            dap = dest_bf[:, g * 2048:(g + 1) * 2048]
            SC(lambda e, a=dap, b=cw: e.activation(a, b, AF.Copy),
               rd=[wb], wr=[bf(tag)])
        S.barrier()

    def build_lnxT(dsrc, s0):
        for u in range(4):
            par = u % 2
            bx = bf(f"xbuf{par}")
            xap = Df[:, par * 2048:par * 2048 + 2048]
            x2 = Cf[:, 2048:4096]
            lnap = Cbf[:, par * 2048:par * 2048 + 2048]
            lnb = bf(f"lnx{par}")
            LD(xap, dsrc[s0 + u * 128:s0 + u * 128 + 128, :], wr=[bx])
            layer_norm(xap, x2, lnap, bx, bf("x2"), lnb, par)
            for g in range(16):
                pt = pbf[g % 2][:, 0:128]
                bpt = bf(f"pb{g % 2}")
                TR(pt, lnap[:, g * 128:(g + 1) * 128], idb_sb,
                   rd=[lnb, bID], wr=[bpt])
                dst = Dbf[:, 8192 + g * 512 + u * 128:
                          8192 + g * 512 + u * 128 + 128]
                if g % 2 == 0:
                    V(lambda e, a=dst, b=pt: e.tensor_copy(a, b),
                      rd=[bpt], wr=[bf("lnxT")])
                else:
                    SC(lambda e, a=dst, b=pt: e.activation(a, b, AF.Copy),
                       rd=[bpt], wr=[bf("lnxT")])

    lnxT = lambda g: Dbf[:, 8192 + g * 512:8192 + (g + 1) * 512]

    load_w(d_wk, Abf, "WK")
    for sti in range(16):
        build_lnxT(d_xkv, sti * 512)
        for h in range(H):
            pp = psf[h % 2]
            bpp = bf(f"ps{h % 2}")
            for g in range(16):
                MM(pp, Abf[:, g * 2048 + h * 128:g * 2048 + h * 128 + 128],
                   lnxT(g), g == 0, g == 15,
                   rd=[bf("WK"), bf("lnxT")], wr=[bpp])
            par = h % 2
            sq_ap = Ebf[:, 2048 + par * 512:2048 + par * 512 + 512]
            kout = Ebf[:, par * 512:par * 512 + 512]
            bko = bf(f"kout{par}")
            rms_apply(pp, bpp, kout, bko, sq_ap, bf(f"sq{par}"),
                      kw_c, bCOL, 512)
            ST(d_kts[h, :, sti * 512:(sti + 1) * 512], kout, rd=[bko])
    S.barrier()

    load_w(d_wv, Abf, "WV")
    for sti in range(16):
        build_lnxT(d_xkv, sti * 512)
        for grp in range(4):
            for u in range(4):
                pp = psf[2 + (u % 2)]
                bpp = bf(f"ps{2 + (u % 2)}")
                for g in range(16):
                    MM(pp, Dbf[:, 8192 + g * 512 + u * 128:
                               8192 + g * 512 + u * 128 + 128],
                       Abf[:, g * 2048 + grp * 512:g * 2048 + (grp + 1) * 512],
                       g == 0, g == 15, rd=[bf("WV"), bf("lnxT")], wr=[bpp])
                par = u % 2
                vout = Ebf[:, 3072 + par * 512:3072 + par * 512 + 512]
                bvo = bf(f"vout{par}")
                SC(lambda e, a=vout, b=pp: e.activation(a, b, AF.Copy),
                   rd=[bpp], wr=[bvo])
                for hh in range(4):
                    ST(d_vs[grp * 4 + hh,
                            sti * 512 + u * 128:sti * 512 + u * 128 + 128, :],
                       vout[:, hh * 128:(hh + 1) * 128], rd=[bvo])
    S.barrier()

    load_w(d_wq, Bbf, "WQ")
    for tt in range(4):
        build_lnxT(d_xq, tt * 512)
        for h in range(H):
            pp = psf[h % 2]
            bpp = bf(f"ps{h % 2}")
            for g in range(16):
                MM(pp, Bbf[:, g * 2048 + h * 128:g * 2048 + h * 128 + 128],
                   lnxT(g), g == 0, g == 15,
                   rd=[bf("WQ"), bf("lnxT")], wr=[bpp])
            par = h % 2
            sq_ap = Ebf[:, 2048 + par * 512:2048 + par * 512 + 512]
            qdst = Abf[:, h * 2048 + tt * 512:h * 2048 + (tt + 1) * 512]
            rms_apply(pp, bpp, qdst, bf("QT"), sq_ap, bf(f"sq{par}"),
                      wqs[:, h:h + 1], bf("wqs"), 512)
    S.barrier()

    mst = Cbf[:, 0:8192]
    bM = bf("master")
    for i in range(4):
        LD(mst[:, i * 2048:(i + 1) * 2048],
           d_mst[:, i * 2048:(i + 1) * 2048], wr=[bM])
    kb = lambda i: Bbf[:, i * 8192:(i + 1) * 8192]
    vbuf = Bbf[:, 16384:24576]
    trow = Dbf[:, 0:8192]

    def kload(h):
        LD(kb(h % 2), d_kts[h], wr=[bf(f"kb{h % 2}")])
    def vload(h):
        src = d_vs[h].rearrange("(n p) d -> p n d", p=128)
        dst3 = vbuf.rearrange("p (n d) -> p n d", n=64)
        for i in range(4):
            GLD(dst3[:, i * 16:(i + 1) * 16, :],
                src[:, i * 16:(i + 1) * 16, :], wr=[bf("vb")])
    kload(0)
    vload(0)
    for h in range(H):
        slope = SLOPES[h]
        if h + 1 < H:
            kload(h + 1)
        bkb = bf(f"kb{h % 2}")
        for j in range(NJ):
            jp = j % 2
            ns = j + 1
            qsl = Abf[:, h * 2048 + j * 128:h * 2048 + (j + 1) * 128]
            moff = 7680 - 512 * j
            for s5 in range(ns):
                pp = psf[s5 % 2]
                bpp = bf(f"ps{s5 % 2}")
                MM(pp, qsl, kb(h % 2)[:, s5 * 512:(s5 + 1) * 512], True, True,
                   rd=[bf("QT"), bkb], wr=[bpp])
                tsl = trow[:, s5 * 512:(s5 + 1) * 512]
                msl = mst[:, moff + s5 * 512:moff + s5 * 512 + 512]
                V(lambda e, a=tsl, b=pp, c=msl: e.scalar_tensor_tensor(
                    a, b, 1.0, c, ALU.mult, ALU.add),
                  rd=[bpp, bM], wr=[bf(f"tr{s5}")])
            trb = [bf(f"tr{s5}") for s5 in range(ns)]
            V(lambda e, a=maxc[jp], n=ns: e.reduce_max(
                a, trow[:, 0:n * 512], axis=AX), rd=trb, wr=[bf(f"mx{jp}")])
            V(lambda e, a=negm[jp], b=maxc[jp], sl=slope:
                e.tensor_scalar_mul(a, b, -sl),
              rd=[bf(f"mx{jp}")], wr=[bf(f"mx{jp}")])
            avp = psf[2][:, 0:128]
            bav = bf("ps2")
            first = True
            for s5 in range(ns):
                par = s5 % 2
                esl = Ebf[:, par * 512:par * 512 + 512]
                beb = bf(f"exp{par}")
                SC(lambda e, a=esl, s5=s5, sca=slopes_c[:, h:h + 1], bi=negm[jp]:
                    e.activation(a, trow[:, s5 * 512:(s5 + 1) * 512],
                                 AF.Exp, scale=sca, bias=bi),
                   rd=[bf(f"tr{s5}"), bf(f"mx{jp}")], wr=[beb])
                V(lambda e, a=denp[jp][par], b=esl: e.reduce_sum(a, b, axis=AX),
                  rd=[beb], wr=[bf(f"dp{jp}{par}")])
                if s5 == 0:
                    V(lambda e, a=denc[jp], b=denp[jp][0]: e.tensor_copy(a, b),
                      rd=[bf(f"dp{jp}0")], wr=[bf(f"den{jp}")])
                else:
                    V(lambda e, a=denc[jp], b=denp[jp][par]:
                        e.tensor_add(a, a, b),
                      rd=[bf(f"dp{jp}{par}"), bf(f"den{jp}")],
                      wr=[bf(f"den{jp}")])
                for u in range(4):
                    pt = pbf[u % 2][:, 0:128]
                    bpt = bf(f"pb{u % 2}")
                    TR(pt, esl[:, u * 128:(u + 1) * 128], idb_sb,
                       rd=[beb, bID], wr=[bpt])
                    asl = Ebf[:, 1024 + u * 128:1024 + u * 128 + 128]
                    bas = bf(f"at{u}")
                    SC(lambda e, a=asl, b=pt: e.activation(a, b, AF.Copy),
                       rd=[bpt], wr=[bas])
                    MM(avp, vbuf[:, (s5 * 4 + u) * 128:(s5 * 4 + u) * 128 + 128],
                       asl, first, (s5 == ns - 1 and u == 3),
                       rd=[bf("vb"), bas], wr=[bav])
                    first = False
            av2 = Ebf[:, 1536:1664]
            bv2 = bf("av2")
            SC(lambda e, a=av2, b=avp: e.activation(a, b, AF.Square),
               rd=[bav], wr=[bv2])
            ssq = psf[3][0:1, 0:128]
            bss = bf("ps3")
            MM(ssq, onc_sb, av2, True, True, rd=[bON, bv2], wr=[bss])
            dtp = psf[5][0:1, 0:128]
            bdt = bf("ps5")
            TR(dtp, denc[jp], idf_sb, rd=[bf(f"den{jp}"), bID], wr=[bdt])
            bdr = bf("drow")
            SC(lambda e, a=drow, b=dtp: e.activation(
                a, b, AF.Identity, scale=sqeps_c[0:1, :]), rd=[bdt, bCOL], wr=[bdr])
            V(lambda e, a=t1r, b=drow: e.tensor_mul(a, b, b),
              rd=[bdr], wr=[bdr])
            bms = bf("msrow")
            V(lambda e, a=msr, b=ssq, c=t1r: e.scalar_tensor_tensor(
                a, b, 1.0 / 128.0, c, ALU.mult, ALU.add),
              rd=[bss, bdr], wr=[bms])
            SC(lambda e, a=msr: e.activation(a, a, AF.Sqrt), rd=[bms], wr=[bms])
            V(lambda e, a=rstd2_b, b=msr: e.reciprocal(a, b),
              rd=[bms], wr=[bms])
            bcp = psf[4][:, 0:128]
            bbc = bf("ps4")
            MM(bcp, onr_sb, rstd2_b, True, True, rd=[bON, bms], wr=[bbc])
            bcsb = Ebf[:, 1664:1792]
            bbs = bf("bcsb2")
            SC(lambda e, a=bcsb, b=bcp: e.activation(a, b, AF.Copy),
               rd=[bbc], wr=[bbs])
            yT = Ebf[:, 1792 + jp * 128:1792 + jp * 128 + 128]
            byt = bf(f"yT{jp}")
            V(lambda e, a=yT, b=avp, c=bcsb: e.scalar_tensor_tensor(
                a, b, sw_c, c, ALU.mult, ALU.mult),
              rd=[bav, bbs, bCOL], wr=[byt])
            ST(d_yts[h, :, j * 128:(j + 1) * 128], yT, rd=[byt])
        if h + 1 < H:
            vload(h + 1)
    S.barrier()

    load_w(d_wp, Abf, "WP")
    for j in range(NJ):
        ysl = Dbf[:, 0:2048]
        bys = bf("ysl")
        for h in range(H):
            LD(ysl[:, h * 128:(h + 1) * 128],
               d_yts[h, :, j * 128:(j + 1) * 128], wr=[bys])
        xqt = Df[:, 1024:3072]
        bxq = bf("xqt")
        GLD(xqt, d_xq[j * 128:(j + 1) * 128, :], wr=[bxq])
        res = Df[:, 3072:5120]
        brs = bf("res")
        for cg in range(4):
            pp = psf[cg]
            bpp = bf(f"ps{cg}")
            for h in range(H):
                MM(pp, ysl[:, h * 128:(h + 1) * 128],
                   Abf[:, h * 2048 + cg * 512:h * 2048 + (cg + 1) * 512],
                   h == 0, h == 15, rd=[bf("WP"), bys], wr=[bpp])
            rsl = res[:, cg * 512:(cg + 1) * 512]
            xsl = xqt[:, cg * 512:(cg + 1) * 512]
            V(lambda e, a=rsl, b=pp, c=xsl: e.scalar_tensor_tensor(
                a, b, 1.0, c, ALU.mult, ALU.add), rd=[bpp, bxq], wr=[brs])
        ST(d_xrs[j * 128:(j + 1) * 128, :], res, rd=[brs])
    S.barrier()

    b1cols = Ff[:, 1491:1523]
    b2cols = Ff[:, 1523:1555]
    bstg = Ef[0:32, 0:128]
    bstg2 = Ef[0:32, 128:256]
    bbs_ = bf("bstage")
    LD(bstg, d_b1.rearrange("(a p) -> a p", p=128), wr=[bbs_])
    LD(bstg2, d_b2.rearrange("(a p) -> a p", p=128), wr=[bbs_])
    _p5 = psf[5][:, 0:32]
    TR(_p5, bstg, idf_sb[0:32, 0:32], rd=[bbs_, bID], wr=[bf("ps5")])
    SC(lambda e: e.activation(b1cols, _p5, AF.Copy), rd=[bf("ps5")],
       wr=[bf("bcols")])
    _p52 = psf[5][:, 32:64]
    TR(_p52, bstg2, idf_sb[0:32, 0:32], rd=[bbs_, bID], wr=[bf("ps5")])
    SC(lambda e: e.activation(b2cols, _p52, AF.Copy), rd=[bf("ps5")],
       wr=[bf("bcols")])
    S.barrier()
    for blk in range(2):
        base = blk * 1024
        for u in range(8):
            par = u % 2
            bx = bf(f"fx{par}")
            xap = Bf[:, par * 2048:par * 2048 + 2048]
            x2 = Bf[:, 4096:6144]
            lnap = Bbf[:, 12288 + par * 2048:12288 + par * 2048 + 2048]
            lnb = bf(f"fln{par}")
            LD(xap, d_xrs[base + u * 128:base + u * 128 + 128, :], wr=[bx])
            layer_norm(xap, x2, lnap, bx, bf("fx2"), lnb, par)
            for g in range(16):
                pt = pbf[g % 2][:, 0:128]
                bpt = bf(f"pb{g % 2}")
                TR(pt, lnap[:, g * 128:(g + 1) * 128], idb_sb,
                   rd=[lnb, bID], wr=[bpt])
                dst = Dbf[:, g * 1024 + u * 128:g * 1024 + u * 128 + 128]
                if g % 2 == 0:
                    V(lambda e, a=dst, b=pt: e.tensor_copy(a, b),
                      rd=[bpt], wr=[bf("xlnT")])
                else:
                    SC(lambda e, a=dst, b=pt: e.activation(a, b, AF.Copy),
                       rd=[bpt], wr=[bf("xlnT")])
        S.barrier()
        for d1 in range(32):
            par = d1 % 2
            wrb = Cf[:, par * 2048:par * 2048 + 2048]
            bwr = bf(f"w1r{par}")
            LD(wrb, d_w1[d1 * 128:(d1 + 1) * 128, :], wr=[bwr])
            wbfa = Bbf[:, par * 2048:par * 2048 + 2048]
            bwb = bf(f"w1b{par}")
            SC(lambda e, a=wbfa, b=wrb: e.activation(a, b, AF.Copy),
               rd=[bwr], wr=[bwb])
            w1t = Ebf[:, par * 2048:par * 2048 + 2048]
            bwt = bf(f"w1t{par}")
            for g in range(16):
                pt = pbf[g % 2][:, 0:128]
                bpt = bf(f"pb{g % 2}")
                TR(pt, wbfa[:, g * 128:(g + 1) * 128], idb_sb,
                   rd=[bwb, bID], wr=[bpt])
                V(lambda e, a=w1t[:, g * 128:(g + 1) * 128], b=pt:
                    e.tensor_copy(a, b), rd=[bpt], wr=[bwt])
            bcl = bf("bcols")
            for tt in range(2):
                pp = psf[2 * par + tt]
                bpp = bf(f"ps{2 * par + tt}")
                for g in range(16):
                    MM(pp, w1t[:, g * 128:(g + 1) * 128],
                       Dbf[:, g * 1024 + tt * 512:g * 1024 + (tt + 1) * 512],
                       g == 0, g == 15, rd=[bwt, bf("xlnT")], wr=[bpp])
                hsl = Abf[:, d1 * 1024 + tt * 512:d1 * 1024 + (tt + 1) * 512]
                SC(lambda e, a=hsl, b=pp, bi=b1cols[:, d1:d1 + 1]:
                   e.activation(a, b, AF.Relu, bias=bi),
                   rd=[bpp, bcl], wr=[bf("h1T")])
        S.barrier()
        for d2 in range(32):
            par = d2 % 2
            wrb = Df[:, par * 4096:par * 4096 + 4096]
            bwr = bf(f"w2r{par}")
            (LD if par == 0 else GLD)(wrb, d_w2[d2 * 128:(d2 + 1) * 128, :],
                                      wr=[bwr])
            w2t = Cbf[:, par * 4096:par * 4096 + 4096]
            bwt = bf(f"w2t{par}")
            for half in range(2):
                wbfa = Ebf[:, half * 2048:half * 2048 + 2048]
                bwb = bf(f"w2b{half}")
                SC(lambda e, a=wbfa, b=wrb[:, half * 2048:half * 2048 + 2048]:
                    e.activation(a, b, AF.Copy), rd=[bwr], wr=[bwb])
                for g in range(16):
                    gg = half * 16 + g
                    pt = pbf[g % 2][:, 0:128]
                    bpt = bf(f"pb{g % 2}")
                    TR(pt, wbfa[:, g * 128:(g + 1) * 128], idb_sb,
                       rd=[bwb, bID], wr=[bpt])
                    V(lambda e, a=w2t[:, gg * 128:(gg + 1) * 128], b=pt:
                        e.tensor_copy(a, b), rd=[bpt], wr=[bwt])
            bcl = bf("bcols")
            for tt in range(2):
                pp = psf[2 * par + tt]
                bpp = bf(f"ps{2 * par + tt}")
                for g in range(32):
                    MM(pp, w2t[:, g * 128:(g + 1) * 128],
                       Abf[:, g * 1024 + tt * 512:g * 1024 + (tt + 1) * 512],
                       g == 0, g == 31, rd=[bwt, bf("h1T")], wr=[bpp])
                hsl = Bbf[:, d2 * 1024 + tt * 512:d2 * 1024 + (tt + 1) * 512]
                SC(lambda e, a=hsl, b=pp, bi=b2cols[:, d2:d2 + 1]:
                   e.activation(a, b, AF.Relu, bias=bi),
                   rd=[bpp, bcl], wr=[bf("h2T")])
        S.barrier()
        b3sb = Df[:, 4096:6144]
        b3b = bf("b3sb")
        LD(b3sb, d_b3, wr=[b3b])
        for cg in range(4):
            for th in range(2):
                ppb = [bf(f"ps{i}") for i in range(4)]
                for gg in range(4):
                    w3t = Dbf[:, (gg % 2) * 4096:(gg % 2) * 4096 + 4096]
                    bw3t = bf(f"w3t{gg % 2}")
                    for cs in range(4):
                        par = cs % 2
                        wsl = Cf[:, par * 1024:par * 1024 + 1024]
                        bws = bf(f"w3r{par}")
                        LD(wsl, d_w3[cg * 512 + cs * 128:
                                     cg * 512 + cs * 128 + 128,
                                     gg * 1024:(gg + 1) * 1024], wr=[bws])
                        wbfa = Ebf[:, par * 1024:par * 1024 + 1024]
                        bwb = bf(f"w3b{par}")
                        SC(lambda e, a=wbfa, b=wsl: e.activation(a, b, AF.Copy),
                           rd=[bws], wr=[bwb])
                        for g in range(8):
                            pt = pbf[g % 2][:, 0:128]
                            bpt = bf(f"pb{g % 2}")
                            TR(pt, wbfa[:, g * 128:(g + 1) * 128], idb_sb,
                               rd=[bwb, bID], wr=[bpt])
                            dst = w3t[:, g * 512 + cs * 128:
                                      g * 512 + cs * 128 + 128]
                            V(lambda e, a=dst, b=pt: e.tensor_copy(a, b),
                              rd=[bpt], wr=[bw3t])
                    for tt in range(4):
                        tglob = th * 4 + tt
                        pp = psf[tt]
                        for g in range(8):
                            g2 = gg * 8 + g
                            MM(pp, Bbf[:, g2 * 1024 + tglob * 128:
                                       g2 * 1024 + tglob * 128 + 128],
                               w3t[:, g * 512:(g + 1) * 512],
                               (gg == 0 and g == 0), (gg == 3 and g == 7),
                               rd=[bf("h2T"), bw3t], wr=[ppb[tt]])
                for tt in range(4):
                    tglob = th * 4 + tt
                    rowg = base + tglob * 128
                    xrt = Df[:, 6144 + tt * 512:6144 + (tt + 1) * 512]
                    bxr = bf(f"xr{tt}")
                    GLD(xrt, d_xrs[rowg:rowg + 128, cg * 512:(cg + 1) * 512],
                        wr=[bxr])
                    ot = Cf[:, 2048 + tt * 512:2048 + (tt + 1) * 512]
                    bot = bf(f"ot{tt}")
                    V(lambda e, a=ot, b=psf[tt],
                        c=b3sb[:, cg * 512:(cg + 1) * 512]:
                        e.scalar_tensor_tensor(a, b, 1.0, c, ALU.mult, ALU.add),
                      rd=[ppb[tt], b3b], wr=[bot])
                    GP(lambda e, a=ot, b=xrt: e.tensor_add(a, a, b),
                       rd=[bxr, bot], wr=[bot])
                    ST(d_out[rowg:rowg + 128, cg * 512:(cg + 1) * 512], ot,
                       rd=[bot])
        S.barrier()

    S.q["gpsimd"].append(("w", "st", S.shadow["st"]))

    import contextlib
    lp = lambda: nc.allow_low_precision(reason="bf16 matmul operand prep")
    with nc.Block() as block:
        def runner(name):
            def f(eng):
                with lp():
                    _run(eng)
            def _run(eng):
                for item in S.q[name]:
                    if item[0] == "w":
                        eng.wait_ge(sems[item[1]], item[2])
                    else:
                        ins = item[1](eng)
                        ins.then_inc(sems[item[2]], item[3])
            return f
        block.sync(runner("sync"))
        block.gpsimd(runner("gpsimd"))
        block.tensor(runner("tensor"))
        block.vector(runner("vector"))
        block.scalar(runner("scalar"))

    for cm in reversed(ctx_list):
        cm.__exit__(None, None, None)
    return nc


_cached = {}
def _get_nc():
    if "nc" not in _cached:
        _cached["nc"] = build_nc()
    return _cached["nc"]


def kernel(L0_attn_x, L0_attn_Wq, L0_attn_Wk, L0_attn_Wv, L0_attn_Wproj,
           L0_attn_q_rms_w, L0_attn_k_rms_w, L0_attn_subln_w,
           L0_ffn_W1, L0_ffn_b1, L0_ffn_W2, L0_ffn_b2, L0_ffn_W3, L0_ffn_b3):
    f32 = lambda a: np.ascontiguousarray(np.asarray(a, dtype=np.float32))
    x = f32(L0_attn_x)
    wq, wk, wv, wp = map(f32, (L0_attn_Wq, L0_attn_Wk, L0_attn_Wv,
                               L0_attn_Wproj))
    qw, kw, sw = map(f32, (L0_attn_q_rms_w, L0_attn_k_rms_w, L0_attn_subln_w))
    w1, b1, w2, b2, w3, b3 = map(f32, (L0_ffn_W1, L0_ffn_b1, L0_ffn_W2,
                                       L0_ffn_b2, L0_ffn_W3, L0_ffn_b3))
    b3bc = np.ascontiguousarray(np.broadcast_to(b3[None, :], (128, 2048)))
    idb = np.eye(128, dtype=BFNP)
    idf = np.eye(128, dtype=np.float32)
    onc = np.ones((128, 1), dtype=BFNP)
    onr = np.ones((1, 128), dtype=BFNP)

    masters = []
    for r in range(4):
        w_ = (np.arange(8192, dtype=np.float64)[None, :] - 7680.0 - 128.0 * r
              - np.arange(128, dtype=np.float64)[:, None])
        masters.append(np.ascontiguousarray(
            np.where(w_ <= 0, w_, -1e30).astype(BFNP)))
    idx_all = []
    for c in range(8):
        r = c % 4
        idx = np.concatenate(
            [np.arange((4 * j + r) * 128, (4 * j + r + 1) * 128)
             for j in range(NJ)])
        idx_all.append(idx)

    in_maps = []
    for c in range(8):
        b, r = c // 4, c % 4
        in_maps.append({
            "xkv": x[b], "xq": np.ascontiguousarray(x[b][idx_all[c]]),
            "wq": wq, "wk": wk, "wv": wv, "wp": wp,
            "qw": qw, "kw": kw, "sw": sw,
            "w1": w1, "b1": b1, "w2": w2, "b2": b2, "w3": w3,
            "b3bc": b3bc, "master": masters[r],
            "idbf": idb, "idf32": idf, "onec": onc, "oner": onr,
        })

    nc = _get_nc()
    res = run_bass_kernel_spmd(nc, in_maps, core_ids=list(range(8)))
    out = np.empty((2, 8192, 2048), dtype=np.float32)
    for c in range(8):
        out[c // 4][idx_all[c]] = res.results[c]["out"]
    return out


# version 03
# speedup vs baseline: 1.1924x; 1.1924x over previous
import math
import numpy as np
import ml_dtypes
import concourse.bass as bass
import concourse.mybir as mybir
from concourse.bass_utils import run_bass_kernel_spmd

F32 = mybir.dt.float32
BF16 = mybir.dt.bfloat16
AF = mybir.ActivationFunctionType
ALU = mybir.AluOpType
AX = mybir.AxisListType.X
BFNP = ml_dtypes.bfloat16

B_, T, C, H, D = 2, 8192, 2048, 16, 128
NQ = 2048
NJ = 16
EPS = 1e-5
SLOPES = [2.0 ** (-0.5 * (i + 1)) for i in range(H)]


class Buf:
    __slots__ = ("w", "rs")
    def __init__(self):
        self.w = None
        self.rs = []


class Sched:
    ENG = ("sync", "gpsimd", "tensor", "vector", "scalar")
    def __init__(self):
        self.q = {e: [] for e in self.ENG}
        self.shadow = {"ld": 0, "st": 0, "gp": 0, "t": 0, "v": 0, "s": 0, "ls": 0}
        self.last_wait = {e: {} for e in self.ENG}
        self.bar = []

    def emit(self, eng, sem, inc, build, reads=(), writes=()):
        deps = list(self.bar)
        for b in reads:
            if b.w is not None:
                deps.append(b.w)
        for b in writes:
            if b.w is not None:
                deps.append(b.w)
            deps.extend(b.rs)
        lw = self.last_wait[eng]
        for (ds, dv) in deps:
            if eng == "tensor" and ds == "t":
                continue
            if lw.get(ds, 0) >= dv:
                continue
            lw[ds] = dv
            self.q[eng].append(("w", ds, dv))
        self.shadow[sem] += inc
        self.q[eng].append(("i", build, sem, inc))
        tok = (sem, self.shadow[sem])
        for b in reads:
            b.rs.append(tok)
        for b in writes:
            b.w = tok
            b.rs = []
        return tok

    def barrier(self):
        self.bar = [(s, v) for s, v in self.shadow.items() if v > 0]


def build_nc():
    nc = bass.Bass("TRN2", target_bir_lowering=False)
    S = Sched()

    d_xkv = nc.dram_tensor("xkv", [T, C], F32, kind="ExternalInput").ap()
    d_xq = nc.dram_tensor("xq", [NQ, C], F32, kind="ExternalInput").ap()
    d_wq = nc.dram_tensor("wq", [C, C], F32, kind="ExternalInput").ap()
    d_wk = nc.dram_tensor("wk", [C, C], F32, kind="ExternalInput").ap()
    d_wv = nc.dram_tensor("wv", [C, C], F32, kind="ExternalInput").ap()
    d_wp = nc.dram_tensor("wp", [C, C], F32, kind="ExternalInput").ap()
    d_qw = nc.dram_tensor("qw", [D], F32, kind="ExternalInput").ap()
    d_kw = nc.dram_tensor("kw", [D], F32, kind="ExternalInput").ap()
    d_sw = nc.dram_tensor("sw", [D], F32, kind="ExternalInput").ap()
    d_w1 = nc.dram_tensor("w1", [4096, 2048], F32, kind="ExternalInput").ap()
    d_b1 = nc.dram_tensor("b1", [4096], F32, kind="ExternalInput").ap()
    d_w2 = nc.dram_tensor("w2", [4096, 4096], F32, kind="ExternalInput").ap()
    d_b2 = nc.dram_tensor("b2", [4096], F32, kind="ExternalInput").ap()
    d_w3 = nc.dram_tensor("w3", [2048, 4096], F32, kind="ExternalInput").ap()
    d_b3 = nc.dram_tensor("b3bc", [128, 2048], F32, kind="ExternalInput").ap()
    d_mst = nc.dram_tensor("master", [128, 8192], BF16, kind="ExternalInput").ap()
    d_idb = nc.dram_tensor("idbf", [128, 128], BF16, kind="ExternalInput").ap()
    d_idf = nc.dram_tensor("idf32", [128, 128], F32, kind="ExternalInput").ap()
    d_onc = nc.dram_tensor("onec", [128, 1], BF16, kind="ExternalInput").ap()
    d_onr = nc.dram_tensor("oner", [1, 128], BF16, kind="ExternalInput").ap()
    d_out = nc.dram_tensor("out", [NQ, C], F32, kind="ExternalOutput").ap()
    d_kts = nc.dram_tensor("kts", [H, 128, T], BF16, kind="Internal").ap()
    d_vs = nc.dram_tensor("vs", [H, T, D], BF16, kind="Internal").ap()
    d_yts = nc.dram_tensor("yts", [H, 128, NQ], BF16, kind="Internal").ap()
    d_xrs = nc.dram_tensor("xrs", [NQ, C], F32, kind="Internal").ap()

    ctx_list = []
    def alloc(cm):
        h = cm.__enter__()
        ctx_list.append(cm)
        return h

    A = alloc(nc.sbuf_tensor("Ab", [128, 16384], F32))
    Bt = alloc(nc.sbuf_tensor("Bb", [128, 16384], F32))
    Ct = alloc(nc.sbuf_tensor("Cb", [128, 4096], F32))
    Dt = alloc(nc.sbuf_tensor("Db", [128, 8192], F32))
    Et = alloc(nc.sbuf_tensor("Eb", [128, 2048], F32))
    Ft = alloc(nc.sbuf_tensor("Fb", [128, 1792], F32))
    Af = A[:]; Bf = Bt[:]; Cf = Ct[:]; Df = Dt[:]; Ef = Et[:]; Ff = Ft[:]
    Abf = Af.bitcast(BF16); Bbf = Bf.bitcast(BF16); Cbf = Cf.bitcast(BF16)
    Dbf = Df.bitcast(BF16); Ebf = Ef.bitcast(BF16); Fbf = Ff.bitcast(BF16)

    ps = [alloc(nc.psum_tensor(f"ps{i}", [128, 512], F32)) for i in range(6)]
    pbm = [alloc(nc.psum_tensor(f"pb{i}", [128, 1024], BF16)) for i in range(2)]
    psf = [p[:] for p in ps]
    pbf = [p[:] for p in pbm]

    sems = {s: alloc(nc.semaphore(f"sem_{s}"))
            for s in ("ld", "st", "gp", "t", "v", "s", "ls")}

    bufs = {}
    def bf(name):
        if name not in bufs:
            bufs[name] = Buf()
        return bufs[name]

    def LD(dst, src, rd=(), wr=()):
        return S.emit("sync", "ld", 16,
                      lambda e, dst=dst, src=src: e.dma_start(out=dst, in_=src),
                      reads=rd, writes=wr)
    def GLD(dst, src, rd=(), wr=()):
        return S.emit("gpsimd", "st", 16,
                      lambda e, dst=dst, src=src: e.dma_start(out=dst, in_=src),
                      reads=rd, writes=wr)
    ST = GLD
    def MM(out, l, rh, st_, sp, rd=(), wr=()):
        return S.emit("tensor", "t", 1,
                      lambda e, out=out, l=l, rh=rh, st_=st_, sp=sp:
                      e.matmul(out, l, rh, start=st_, stop=sp,
                               skip_group_check=True),
                      reads=rd, writes=wr)
    def TR(out, in_, ident, rd=(), wr=()):
        return S.emit("tensor", "t", 1,
                      lambda e, out=out, in_=in_, ident=ident:
                      e.transpose(out, in_, ident),
                      reads=rd, writes=wr)
    def V(buildf, rd=(), wr=()):
        return S.emit("vector", "v", 1, buildf, reads=rd, writes=wr)
    def SC(buildf, rd=(), wr=()):
        return S.emit("scalar", "s", 1, buildf, reads=rd, writes=wr)
    def GP(buildf, rd=(), wr=()):
        return S.emit("gpsimd", "gp", 1, buildf, reads=rd, writes=wr)

    qw_c = Ff[:, 0:1]; kw_c = Ff[:, 1:2]; sw_c = Ff[:, 2:3]
    wqs = Ff[:, 3:19]
    maxc = [Ff[:, 19:20], Ff[:, 28:29]]
    negm = [Ff[:, 20:21], Ff[:, 29:30]]
    denc = [Ff[:, 21:22], Ff[:, 30:31]]
    denp = [[Ff[:, 22:23], Ff[:, 23:24]], [Ff[:, 31:32], Ff[:, 32:33]]]
    b1c_ = [Ff[:, 24:25], Ff[:, 25:26]]
    b2c_ = [Ff[:, 26:27], Ff[:, 27:28]]
    LNC = [(33, 34, 35, 36, 37, 38, 39), (40, 41, 42, 43, 44, 45, 46)]
    idf_sb = Ff[:, 48:176]
    rstd1_b = Fbf[0:1, 352:864]
    rstd2_b = Fbf[0:1, 352:480]
    drow = Ff[0:1, 560:688]
    t1r = Ff[0:1, 688:816]
    msr = Ff[0:1, 816:944]
    rowS = Ff[0:1, 960:1472]
    idb_sb = Fbf[:, 3168:3296]
    onc_sb = Fbf[:, 3296:3297]
    onr_sb = Fbf[0:1, 3298:3426]

    eps_c = Ff[:, 1472:1473]
    sqeps_c = Ff[:, 1473:1474]
    i128_c = Ff[:, 1474:1475]
    slopes_c = Ff[:, 1475:1491]

    bID = bf("id"); bON = bf("ones"); bCOL = bf("cols")
    V(lambda e: e.memset(eps_c, EPS), wr=[bCOL])
    V(lambda e: e.memset(sqeps_c, float(math.sqrt(EPS))), wr=[bCOL])
    V(lambda e: e.memset(i128_c, 1.0 / 128.0), wr=[bCOL])
    for h in range(H):
        V(lambda e, h=h: e.memset(slopes_c[:, h:h + 1], float(SLOPES[h])),
          wr=[bCOL])
    LD(idb_sb, d_idb, wr=[bID])
    LD(idf_sb, d_idf, wr=[bID])
    LD(onc_sb, d_onc, wr=[bON])
    LD(onr_sb, d_onr, wr=[bON])
    LD(qw_c, d_qw[:, None], wr=[bCOL])
    LD(kw_c, d_kw[:, None], wr=[bCOL])
    LD(sw_c, d_sw[:, None], wr=[bCOL])
    for h in range(H):
        sc_h = 1.0 / (math.sqrt(D) * SLOPES[h])
        V(lambda e, h=h, sc_h=sc_h: e.tensor_scalar_mul(
            wqs[:, h:h + 1], qw_c, sc_h), rd=[bCOL], wr=[bf("wqs")])

    def layer_norm(xap, x2ap, outap, bx, bx2, bout, par):
        (c0, c1, c2, c3, c4, c5, c6) = LNC[par]
        cS = Ff[:, c0:c0 + 1]; cQ = Ff[:, c1:c1 + 1]; cM = Ff[:, c2:c2 + 1]
        cM2 = Ff[:, c3:c3 + 1]; cVar = Ff[:, c4:c4 + 1]
        cStd = Ff[:, c5:c5 + 1]; cNmr = Ff[:, c6:c6 + 1]
        bc = bf(f"lncol{par}")
        V(lambda e, a=cS, b=xap: e.reduce_sum(a, b, axis=AX), rd=[bx], wr=[bc])
        SC(lambda e, a=x2ap, b=xap: e.activation(a, b, AF.Square),
           rd=[bx], wr=[bx2])
        V(lambda e, a=cQ, b=x2ap: e.reduce_sum(a, b, axis=AX), rd=[bx2], wr=[bc])
        V(lambda e, a=cM, b=cS: e.tensor_scalar_mul(a, b, 1.0 / 2048.0),
          rd=[bc], wr=[bc])
        V(lambda e, a=cM2, b=cM: e.tensor_mul(a, b, b), rd=[bc], wr=[bc])
        V(lambda e, a=cVar, b=cQ, c=cM2: e.scalar_tensor_tensor(
            a, b, 1.0 / 2048.0, c, ALU.mult, ALU.subtract), rd=[bc], wr=[bc])
        SC(lambda e, a=cStd, b=cVar: e.activation(a, b, AF.Sqrt, bias=eps_c),
           rd=[bc, bCOL], wr=[bc])
        V(lambda e, a=cStd: e.reciprocal(a, a), rd=[bc], wr=[bc])
        V(lambda e, a=cNmr, b=cM, c=cStd: e.scalar_tensor_tensor(
            a, b, -1.0, c, ALU.mult, ALU.mult), rd=[bc], wr=[bc])
        SC(lambda e, a=outap, b=xap, bi=cNmr, sc=cStd: e.activation(
            a, b, AF.Identity, bias=bi, scale=sc), rd=[bx, bc], wr=[bout])

    def rms_apply(pp, bpp, outap, bout, sq_ap, bsq, w_col, bw, n):
        ssq = psf[3][0:1, 0:n]
        bss = bf("ps3")
        bro = bf("rmsrow")
        rS = rowS[:, 0:n]
        rB = rstd1_b[:, 0:n]
        bcsb = Ef[:, 512:1024][:, 0:n]
        bbcs = bf("bcsb1")
        SC(lambda e, a=sq_ap, b=pp: e.activation(a, b, AF.Square),
           rd=[bpp], wr=[bsq])
        MM(ssq, onc_sb, sq_ap, True, True, rd=[bON, bsq], wr=[bss])
        SC(lambda e, a=rS, b=ssq: e.activation(
            a, b, AF.Sqrt, scale=i128_c[0:1, :], bias=eps_c[0:1, :]), rd=[bss, bCOL], wr=[bro])
        V(lambda e, a=rB, b=rS: e.reciprocal(a, b), rd=[bro], wr=[bro])
        bcp = psf[4][:, 0:n]
        bbc = bf("ps4")
        MM(bcp, onr_sb, rB, True, True, rd=[bON, bro], wr=[bbc])
        SC(lambda e, a=bcsb, b=bcp: e.activation(a, b, AF.Copy),
           rd=[bbc], wr=[bbcs])
        V(lambda e, a=outap, b=pp, w=w_col, c=bcsb: e.scalar_tensor_tensor(
            a, b, w, c, ALU.mult, ALU.mult), rd=[bpp, bbcs, bw], wr=[bout])

    def load_w(dsrc, dest_bf, tag):
        for g in range(16):
            wb = bf(f"wload{g % 2}")
            cw = Cf[:, (g % 2) * 2048:(g % 2) * 2048 + 2048]
            LD(cw, dsrc[g * 128:(g + 1) * 128, :], wr=[wb])
            dap = dest_bf[:, g * 2048:(g + 1) * 2048]
            SC(lambda e, a=dap, b=cw: e.activation(a, b, AF.Copy),
               rd=[wb], wr=[bf(tag)])
        S.barrier()

    def build_lnxT(dsrc, s0):
        for u in range(4):
            par = u % 2
            bx = bf(f"xbuf{par}")
            xap = Df[:, par * 2048:par * 2048 + 2048]
            x2 = Cf[:, 2048:4096]
            lnap = Cbf[:, par * 2048:par * 2048 + 2048]
            lnb = bf(f"lnx{par}")
            LD(xap, dsrc[s0 + u * 128:s0 + u * 128 + 128, :], wr=[bx])
            layer_norm(xap, x2, lnap, bx, bf("x2"), lnb, par)
            for g in range(16):
                pt = pbf[g % 2][:, 0:128]
                bpt = bf(f"pb{g % 2}")
                TR(pt, lnap[:, g * 128:(g + 1) * 128], idb_sb,
                   rd=[lnb, bID], wr=[bpt])
                dst = Dbf[:, 8192 + g * 512 + u * 128:
                          8192 + g * 512 + u * 128 + 128]
                if g % 2 == 0:
                    V(lambda e, a=dst, b=pt: e.tensor_copy(a, b),
                      rd=[bpt], wr=[bf("lnxT")])
                else:
                    SC(lambda e, a=dst, b=pt: e.activation(a, b, AF.Copy),
                       rd=[bpt], wr=[bf("lnxT")])

    lnxT = lambda g: Dbf[:, 8192 + g * 512:8192 + (g + 1) * 512]

    load_w(d_wk, Abf, "WK")
    for sti in range(16):
        build_lnxT(d_xkv, sti * 512)
        for h in range(H):
            pp = psf[h % 2]
            bpp = bf(f"ps{h % 2}")
            for g in range(16):
                MM(pp, Abf[:, g * 2048 + h * 128:g * 2048 + h * 128 + 128],
                   lnxT(g), g == 0, g == 15,
                   rd=[bf("WK"), bf("lnxT")], wr=[bpp])
            par = h % 2
            sq_ap = Ebf[:, 2048 + par * 512:2048 + par * 512 + 512]
            kout = Ebf[:, par * 512:par * 512 + 512]
            bko = bf(f"kout{par}")
            rms_apply(pp, bpp, kout, bko, sq_ap, bf(f"sq{par}"),
                      kw_c, bCOL, 512)
            ST(d_kts[h, :, sti * 512:(sti + 1) * 512], kout, rd=[bko])
    S.barrier()

    load_w(d_wv, Abf, "WV")
    for sti in range(16):
        build_lnxT(d_xkv, sti * 512)
        for grp in range(4):
            for u in range(4):
                pp = psf[2 + (u % 2)]
                bpp = bf(f"ps{2 + (u % 2)}")
                for g in range(16):
                    MM(pp, Dbf[:, 8192 + g * 512 + u * 128:
                               8192 + g * 512 + u * 128 + 128],
                       Abf[:, g * 2048 + grp * 512:g * 2048 + (grp + 1) * 512],
                       g == 0, g == 15, rd=[bf("WV"), bf("lnxT")], wr=[bpp])
                par = u % 2
                vout = Ebf[:, 3072 + par * 512:3072 + par * 512 + 512]
                bvo = bf(f"vout{par}")
                SC(lambda e, a=vout, b=pp: e.activation(a, b, AF.Copy),
                   rd=[bpp], wr=[bvo])
                for hh in range(4):
                    ST(d_vs[grp * 4 + hh,
                            sti * 512 + u * 128:sti * 512 + u * 128 + 128, :],
                       vout[:, hh * 128:(hh + 1) * 128], rd=[bvo])
    S.barrier()

    load_w(d_wq, Bbf, "WQ")
    for tt in range(4):
        build_lnxT(d_xq, tt * 512)
        for h in range(H):
            pp = psf[h % 2]
            bpp = bf(f"ps{h % 2}")
            for g in range(16):
                MM(pp, Bbf[:, g * 2048 + h * 128:g * 2048 + h * 128 + 128],
                   lnxT(g), g == 0, g == 15,
                   rd=[bf("WQ"), bf("lnxT")], wr=[bpp])
            par = h % 2
            sq_ap = Ebf[:, 2048 + par * 512:2048 + par * 512 + 512]
            qdst = Abf[:, h * 2048 + tt * 512:h * 2048 + (tt + 1) * 512]
            rms_apply(pp, bpp, qdst, bf("QT"), sq_ap, bf(f"sq{par}"),
                      wqs[:, h:h + 1], bf("wqs"), 512)
    S.barrier()

    mst = Cbf[:, 0:8192]
    bM = bf("master")
    for i in range(4):
        LD(mst[:, i * 2048:(i + 1) * 2048],
           d_mst[:, i * 2048:(i + 1) * 2048], wr=[bM])
    kb = lambda i: Bbf[:, i * 8192:(i + 1) * 8192]
    vbuf = Bbf[:, 16384:24576]
    trow = Dbf[:, 0:8192]

    def kload(h):
        LD(kb(h % 2), d_kts[h], wr=[bf(f"kb{h % 2}")])
    def vload(h):
        src = d_vs[h].rearrange("(n p) d -> p n d", p=128)
        dst3 = vbuf.rearrange("p (n d) -> p n d", n=64)
        for i in range(4):
            GLD(dst3[:, i * 16:(i + 1) * 16, :],
                src[:, i * 16:(i + 1) * 16, :], wr=[bf("vb")])
    kload(0)
    vload(0)
    for h in range(H):
        slope = SLOPES[h]
        if h + 1 < H:
            kload(h + 1)
        bkb = bf(f"kb{h % 2}")
        for j in range(NJ):
            jp = j % 2
            ns = j + 1
            qsl = Abf[:, h * 2048 + j * 128:h * 2048 + (j + 1) * 128]
            moff = 7680 - 512 * j
            for s5 in range(ns):
                pp = psf[s5 % 2]
                bpp = bf(f"ps{s5 % 2}")
                MM(pp, qsl, kb(h % 2)[:, s5 * 512:(s5 + 1) * 512], True, True,
                   rd=[bf("QT"), bkb], wr=[bpp])
                tsl = trow[:, s5 * 512:(s5 + 1) * 512]
                msl = mst[:, moff + s5 * 512:moff + s5 * 512 + 512]
                V(lambda e, a=tsl, b=pp, c=msl: e.scalar_tensor_tensor(
                    a, b, 1.0, c, ALU.mult, ALU.add),
                  rd=[bpp, bM], wr=[bf(f"tr{s5}")])
            trb = [bf(f"tr{s5}") for s5 in range(ns)]
            V(lambda e, a=maxc[jp], n=ns: e.reduce_max(
                a, trow[:, 0:n * 512], axis=AX), rd=trb, wr=[bf(f"mx{jp}")])
            V(lambda e, a=negm[jp], b=maxc[jp], sl=slope:
                e.tensor_scalar_mul(a, b, -sl),
              rd=[bf(f"mx{jp}")], wr=[bf(f"mx{jp}")])
            avp = psf[2][:, 0:128]
            bav = bf("ps2")
            first = True
            for s5 in range(ns):
                par = s5 % 2
                esl = Ebf[:, par * 512:par * 512 + 512]
                beb = bf(f"exp{par}")
                SC(lambda e, a=esl, s5=s5, sca=slopes_c[:, h:h + 1], bi=negm[jp]:
                    e.activation(a, trow[:, s5 * 512:(s5 + 1) * 512],
                                 AF.Exp, scale=sca, bias=bi),
                   rd=[bf(f"tr{s5}"), bf(f"mx{jp}")], wr=[beb])
                V(lambda e, a=denp[jp][par], b=esl: e.reduce_sum(a, b, axis=AX),
                  rd=[beb], wr=[bf(f"dp{jp}{par}")])
                if s5 == 0:
                    V(lambda e, a=denc[jp], b=denp[jp][0]: e.tensor_copy(a, b),
                      rd=[bf(f"dp{jp}0")], wr=[bf(f"den{jp}")])
                else:
                    V(lambda e, a=denc[jp], b=denp[jp][par]:
                        e.tensor_add(a, a, b),
                      rd=[bf(f"dp{jp}{par}"), bf(f"den{jp}")],
                      wr=[bf(f"den{jp}")])
                asls = []
                for u in range(4):
                    pt = pbf[u % 2][:, 0:128]
                    bpt = bf(f"pb{u % 2}")
                    TR(pt, esl[:, u * 128:(u + 1) * 128], idb_sb,
                       rd=[beb, bID], wr=[bpt])
                    asl = Ebf[:, 1024 + u * 128:1024 + u * 128 + 128]
                    bas = bf(f"at{u}")
                    SC(lambda e, a=asl, b=pt: e.activation(a, b, AF.Copy),
                       rd=[bpt], wr=[bas])
                    asls.append((asl, bas))
                for u in range(4):
                    asl, bas = asls[u]
                    MM(avp, vbuf[:, (s5 * 4 + u) * 128:(s5 * 4 + u) * 128 + 128],
                       asl, first, (s5 == ns - 1 and u == 3),
                       rd=[bf("vb"), bas], wr=[bav])
                    first = False
            av2 = Ebf[:, 1536:1664]
            bv2 = bf("av2")
            SC(lambda e, a=av2, b=avp: e.activation(a, b, AF.Square),
               rd=[bav], wr=[bv2])
            ssq = psf[3][0:1, 0:128]
            bss = bf("ps3")
            MM(ssq, onc_sb, av2, True, True, rd=[bON, bv2], wr=[bss])
            dtp = psf[5][0:1, 0:128]
            bdt = bf("ps5")
            TR(dtp, denc[jp], idf_sb, rd=[bf(f"den{jp}"), bID], wr=[bdt])
            bdr = bf("drow")
            SC(lambda e, a=drow, b=dtp: e.activation(
                a, b, AF.Identity, scale=sqeps_c[0:1, :]), rd=[bdt, bCOL], wr=[bdr])
            V(lambda e, a=t1r, b=drow: e.tensor_mul(a, b, b),
              rd=[bdr], wr=[bdr])
            bms = bf("msrow")
            V(lambda e, a=msr, b=ssq, c=t1r: e.scalar_tensor_tensor(
                a, b, 1.0 / 128.0, c, ALU.mult, ALU.add),
              rd=[bss, bdr], wr=[bms])
            SC(lambda e, a=msr: e.activation(a, a, AF.Sqrt), rd=[bms], wr=[bms])
            V(lambda e, a=rstd2_b, b=msr: e.reciprocal(a, b),
              rd=[bms], wr=[bms])
            bcp = psf[4][:, 0:128]
            bbc = bf("ps4")
            MM(bcp, onr_sb, rstd2_b, True, True, rd=[bON, bms], wr=[bbc])
            bcsb = Ebf[:, 1664:1792]
            bbs = bf("bcsb2")
            SC(lambda e, a=bcsb, b=bcp: e.activation(a, b, AF.Copy),
               rd=[bbc], wr=[bbs])
            yT = Ebf[:, 1792 + jp * 128:1792 + jp * 128 + 128]
            byt = bf(f"yT{jp}")
            V(lambda e, a=yT, b=avp, c=bcsb: e.scalar_tensor_tensor(
                a, b, sw_c, c, ALU.mult, ALU.mult),
              rd=[bav, bbs, bCOL], wr=[byt])
            ST(d_yts[h, :, j * 128:(j + 1) * 128], yT, rd=[byt])
        if h + 1 < H:
            vload(h + 1)
    S.barrier()

    load_w(d_wp, Abf, "WP")
    for j in range(NJ):
        ysl = Dbf[:, 0:2048]
        bys = bf("ysl")
        for h in range(H):
            LD(ysl[:, h * 128:(h + 1) * 128],
               d_yts[h, :, j * 128:(j + 1) * 128], wr=[bys])
        xqt = Df[:, 1024:3072]
        bxq = bf("xqt")
        GLD(xqt, d_xq[j * 128:(j + 1) * 128, :], wr=[bxq])
        res = Df[:, 3072:5120]
        brs = bf("res")
        for cg in range(4):
            pp = psf[cg]
            bpp = bf(f"ps{cg}")
            for h in range(H):
                MM(pp, ysl[:, h * 128:(h + 1) * 128],
                   Abf[:, h * 2048 + cg * 512:h * 2048 + (cg + 1) * 512],
                   h == 0, h == 15, rd=[bf("WP"), bys], wr=[bpp])
            rsl = res[:, cg * 512:(cg + 1) * 512]
            xsl = xqt[:, cg * 512:(cg + 1) * 512]
            V(lambda e, a=rsl, b=pp, c=xsl: e.scalar_tensor_tensor(
                a, b, 1.0, c, ALU.mult, ALU.add), rd=[bpp, bxq], wr=[brs])
        ST(d_xrs[j * 128:(j + 1) * 128, :], res, rd=[brs])
    S.barrier()

    b1cols = Ff[:, 1491:1523]
    b2cols = Ff[:, 1523:1555]
    bstg = Ef[0:32, 0:128]
    bstg2 = Ef[0:32, 128:256]
    bbs_ = bf("bstage")
    LD(bstg, d_b1.rearrange("(a p) -> a p", p=128), wr=[bbs_])
    LD(bstg2, d_b2.rearrange("(a p) -> a p", p=128), wr=[bbs_])
    _p5 = psf[5][:, 0:32]
    TR(_p5, bstg, idf_sb[0:32, 0:32], rd=[bbs_, bID], wr=[bf("ps5")])
    SC(lambda e: e.activation(b1cols, _p5, AF.Copy), rd=[bf("ps5")],
       wr=[bf("bcols")])
    _p52 = psf[5][:, 32:64]
    TR(_p52, bstg2, idf_sb[0:32, 0:32], rd=[bbs_, bID], wr=[bf("ps5")])
    SC(lambda e: e.activation(b2cols, _p52, AF.Copy), rd=[bf("ps5")],
       wr=[bf("bcols")])
    S.barrier()
    for blk in range(2):
        base = blk * 1024
        for u in range(8):
            par = u % 2
            bx = bf(f"fx{par}")
            xap = Bf[:, par * 2048:par * 2048 + 2048]
            x2 = Bf[:, 4096:6144]
            lnap = Bbf[:, 12288 + par * 2048:12288 + par * 2048 + 2048]
            lnb = bf(f"fln{par}")
            LD(xap, d_xrs[base + u * 128:base + u * 128 + 128, :], wr=[bx])
            layer_norm(xap, x2, lnap, bx, bf("fx2"), lnb, par)
            for g in range(16):
                pt = pbf[g % 2][:, 0:128]
                bpt = bf(f"pb{g % 2}")
                TR(pt, lnap[:, g * 128:(g + 1) * 128], idb_sb,
                   rd=[lnb, bID], wr=[bpt])
                dst = Dbf[:, g * 1024 + u * 128:g * 1024 + u * 128 + 128]
                if g % 2 == 0:
                    V(lambda e, a=dst, b=pt: e.tensor_copy(a, b),
                      rd=[bpt], wr=[bf("xlnT")])
                else:
                    SC(lambda e, a=dst, b=pt: e.activation(a, b, AF.Copy),
                       rd=[bpt], wr=[bf("xlnT")])
        S.barrier()
        for d1 in range(32):
            par = d1 % 2
            wrb = Cf[:, par * 2048:par * 2048 + 2048]
            bwr = bf(f"w1r{par}")
            LD(wrb, d_w1[d1 * 128:(d1 + 1) * 128, :], wr=[bwr])
            wbfa = Bbf[:, par * 2048:par * 2048 + 2048]
            bwb = bf(f"w1b{par}")
            SC(lambda e, a=wbfa, b=wrb: e.activation(a, b, AF.Copy),
               rd=[bwr], wr=[bwb])
            w1t = Ebf[:, par * 2048:par * 2048 + 2048]
            bwt = bf(f"w1t{par}")
            for g in range(16):
                pt = pbf[g % 2][:, 0:128]
                bpt = bf(f"pb{g % 2}")
                TR(pt, wbfa[:, g * 128:(g + 1) * 128], idb_sb,
                   rd=[bwb, bID], wr=[bpt])
                V(lambda e, a=w1t[:, g * 128:(g + 1) * 128], b=pt:
                    e.tensor_copy(a, b), rd=[bpt], wr=[bwt])
            bcl = bf("bcols")
            for tt in range(2):
                pp = psf[2 * par + tt]
                bpp = bf(f"ps{2 * par + tt}")
                for g in range(16):
                    MM(pp, w1t[:, g * 128:(g + 1) * 128],
                       Dbf[:, g * 1024 + tt * 512:g * 1024 + (tt + 1) * 512],
                       g == 0, g == 15, rd=[bwt, bf("xlnT")], wr=[bpp])
                hsl = Abf[:, d1 * 1024 + tt * 512:d1 * 1024 + (tt + 1) * 512]
                SC(lambda e, a=hsl, b=pp, bi=b1cols[:, d1:d1 + 1]:
                   e.activation(a, b, AF.Relu, bias=bi),
                   rd=[bpp, bcl], wr=[bf("h1T")])
        S.barrier()
        for d2 in range(32):
            par = d2 % 2
            wrb = Df[:, par * 4096:par * 4096 + 4096]
            bwr = bf(f"w2r{par}")
            (LD if par == 0 else GLD)(wrb, d_w2[d2 * 128:(d2 + 1) * 128, :],
                                      wr=[bwr])
            w2t = Cbf[:, par * 4096:par * 4096 + 4096]
            bwt = bf(f"w2t{par}")
            for half in range(2):
                wbfa = Ebf[:, half * 2048:half * 2048 + 2048]
                bwb = bf(f"w2b{half}")
                SC(lambda e, a=wbfa, b=wrb[:, half * 2048:half * 2048 + 2048]:
                    e.activation(a, b, AF.Copy), rd=[bwr], wr=[bwb])
                for g in range(16):
                    gg = half * 16 + g
                    pt = pbf[g % 2][:, 0:128]
                    bpt = bf(f"pb{g % 2}")
                    TR(pt, wbfa[:, g * 128:(g + 1) * 128], idb_sb,
                       rd=[bwb, bID], wr=[bpt])
                    V(lambda e, a=w2t[:, gg * 128:(gg + 1) * 128], b=pt:
                        e.tensor_copy(a, b), rd=[bpt], wr=[bwt])
            bcl = bf("bcols")
            for tt in range(2):
                pp = psf[2 * par + tt]
                bpp = bf(f"ps{2 * par + tt}")
                for g in range(32):
                    MM(pp, w2t[:, g * 128:(g + 1) * 128],
                       Abf[:, g * 1024 + tt * 512:g * 1024 + (tt + 1) * 512],
                       g == 0, g == 31, rd=[bwt, bf("h1T")], wr=[bpp])
                hsl = Bbf[:, d2 * 1024 + tt * 512:d2 * 1024 + (tt + 1) * 512]
                SC(lambda e, a=hsl, b=pp, bi=b2cols[:, d2:d2 + 1]:
                   e.activation(a, b, AF.Relu, bias=bi),
                   rd=[bpp, bcl], wr=[bf("h2T")])
        S.barrier()
        b3sb = Df[:, 4096:6144]
        b3b = bf("b3sb")
        LD(b3sb, d_b3, wr=[b3b])
        for cg in range(4):
            for th in range(2):
                ppb = [bf(f"ps{i}") for i in range(4)]
                for gg in range(4):
                    w3t = Dbf[:, (gg % 2) * 4096:(gg % 2) * 4096 + 4096]
                    bw3t = bf(f"w3t{gg % 2}")
                    for cs in range(4):
                        par = cs % 2
                        wsl = Cf[:, par * 1024:par * 1024 + 1024]
                        bws = bf(f"w3r{par}")
                        LD(wsl, d_w3[cg * 512 + cs * 128:
                                     cg * 512 + cs * 128 + 128,
                                     gg * 1024:(gg + 1) * 1024], wr=[bws])
                        wbfa = Ebf[:, par * 1024:par * 1024 + 1024]
                        bwb = bf(f"w3b{par}")
                        SC(lambda e, a=wbfa, b=wsl: e.activation(a, b, AF.Copy),
                           rd=[bws], wr=[bwb])
                        for g in range(8):
                            pt = pbf[g % 2][:, 0:128]
                            bpt = bf(f"pb{g % 2}")
                            TR(pt, wbfa[:, g * 128:(g + 1) * 128], idb_sb,
                               rd=[bwb, bID], wr=[bpt])
                            dst = w3t[:, g * 512 + cs * 128:
                                      g * 512 + cs * 128 + 128]
                            V(lambda e, a=dst, b=pt: e.tensor_copy(a, b),
                              rd=[bpt], wr=[bw3t])
                    for tt in range(4):
                        tglob = th * 4 + tt
                        pp = psf[tt]
                        for g in range(8):
                            g2 = gg * 8 + g
                            MM(pp, Bbf[:, g2 * 1024 + tglob * 128:
                                       g2 * 1024 + tglob * 128 + 128],
                               w3t[:, g * 512:(g + 1) * 512],
                               (gg == 0 and g == 0), (gg == 3 and g == 7),
                               rd=[bf("h2T"), bw3t], wr=[ppb[tt]])
                for tt in range(4):
                    tglob = th * 4 + tt
                    rowg = base + tglob * 128
                    xrt = Df[:, 6144 + tt * 512:6144 + (tt + 1) * 512]
                    bxr = bf(f"xr{tt}")
                    GLD(xrt, d_xrs[rowg:rowg + 128, cg * 512:(cg + 1) * 512],
                        wr=[bxr])
                    ot = Cf[:, 2048 + tt * 512:2048 + (tt + 1) * 512]
                    bot = bf(f"ot{tt}")
                    V(lambda e, a=ot, b=psf[tt],
                        c=b3sb[:, cg * 512:(cg + 1) * 512]:
                        e.scalar_tensor_tensor(a, b, 1.0, c, ALU.mult, ALU.add),
                      rd=[ppb[tt], b3b], wr=[bot])
                    GP(lambda e, a=ot, b=xrt: e.tensor_add(a, a, b),
                       rd=[bxr, bot], wr=[bot])
                    ST(d_out[rowg:rowg + 128, cg * 512:(cg + 1) * 512], ot,
                       rd=[bot])
        S.barrier()

    S.q["gpsimd"].append(("w", "st", S.shadow["st"]))

    import contextlib
    lp = lambda: nc.allow_low_precision(reason="bf16 matmul operand prep")
    with nc.Block() as block:
        def runner(name):
            def f(eng):
                with lp():
                    _run(eng)
            def _run(eng):
                for item in S.q[name]:
                    if item[0] == "w":
                        eng.wait_ge(sems[item[1]], item[2])
                    else:
                        ins = item[1](eng)
                        ins.then_inc(sems[item[2]], item[3])
            return f
        block.sync(runner("sync"))
        block.gpsimd(runner("gpsimd"))
        block.tensor(runner("tensor"))
        block.vector(runner("vector"))
        block.scalar(runner("scalar"))

    for cm in reversed(ctx_list):
        cm.__exit__(None, None, None)
    return nc


_cached = {}
def _get_nc():
    if "nc" not in _cached:
        _cached["nc"] = build_nc()
    return _cached["nc"]


def kernel(L0_attn_x, L0_attn_Wq, L0_attn_Wk, L0_attn_Wv, L0_attn_Wproj,
           L0_attn_q_rms_w, L0_attn_k_rms_w, L0_attn_subln_w,
           L0_ffn_W1, L0_ffn_b1, L0_ffn_W2, L0_ffn_b2, L0_ffn_W3, L0_ffn_b3):
    f32 = lambda a: np.ascontiguousarray(np.asarray(a, dtype=np.float32))
    x = f32(L0_attn_x)
    wq, wk, wv, wp = map(f32, (L0_attn_Wq, L0_attn_Wk, L0_attn_Wv,
                               L0_attn_Wproj))
    qw, kw, sw = map(f32, (L0_attn_q_rms_w, L0_attn_k_rms_w, L0_attn_subln_w))
    w1, b1, w2, b2, w3, b3 = map(f32, (L0_ffn_W1, L0_ffn_b1, L0_ffn_W2,
                                       L0_ffn_b2, L0_ffn_W3, L0_ffn_b3))
    b3bc = np.ascontiguousarray(np.broadcast_to(b3[None, :], (128, 2048)))
    idb = np.eye(128, dtype=BFNP)
    idf = np.eye(128, dtype=np.float32)
    onc = np.ones((128, 1), dtype=BFNP)
    onr = np.ones((1, 128), dtype=BFNP)

    masters = []
    for r in range(4):
        w_ = (np.arange(8192, dtype=np.float64)[None, :] - 7680.0 - 128.0 * r
              - np.arange(128, dtype=np.float64)[:, None])
        masters.append(np.ascontiguousarray(
            np.where(w_ <= 0, w_, -1e30).astype(BFNP)))
    idx_all = []
    for c in range(8):
        r = c % 4
        idx = np.concatenate(
            [np.arange((4 * j + r) * 128, (4 * j + r + 1) * 128)
             for j in range(NJ)])
        idx_all.append(idx)

    in_maps = []
    for c in range(8):
        b, r = c // 4, c % 4
        in_maps.append({
            "xkv": x[b], "xq": np.ascontiguousarray(x[b][idx_all[c]]),
            "wq": wq, "wk": wk, "wv": wv, "wp": wp,
            "qw": qw, "kw": kw, "sw": sw,
            "w1": w1, "b1": b1, "w2": w2, "b2": b2, "w3": w3,
            "b3bc": b3bc, "master": masters[r],
            "idbf": idb, "idf32": idf, "onec": onc, "oner": onr,
        })

    nc = _get_nc()
    res = run_bass_kernel_spmd(nc, in_maps, core_ids=list(range(8)))
    out = np.empty((2, 8192, 2048), dtype=np.float32)
    for c in range(8):
        out[c // 4][idx_all[c]] = res.results[c]["out"]
    return out
